# version 01 (final)
# baseline (speedup 1.0000x reference)
import contextlib

import numpy as np
import ml_dtypes

import concourse.bass as bass
import concourse.mybir as mybir
from concourse.bass_utils import run_bass_kernel_spmd

F32 = mybir.dt.float32
BF16 = mybir.dt.bfloat16
F16 = mybir.dt.float16
AF = mybir.ActivationFunctionType
ALU = mybir.AluOpType
AXX = mybir.AxisListType.X
bfdt = ml_dtypes.bfloat16

H, PD, G, NS, KC, CHK = 128, 64, 8, 128, 4, 128
D, DIN, CONV, PROJ, NKT = 4096, 8192, 10240, 18560, 32
NB_HBC, NB_GATE = CONV // 128, DIN // 128
NBLK = 1 + NB_HBC + NB_GATE
EPS = 1e-5


class Ctx:
    def __init__(self, nc):
        self.nc = nc
        self.st = contextlib.ExitStack()
        self.c = {"g": 0, "h": 0, "pe": 0, "ve": 0, "se": 0, "cc": 0}
        self.sem = {k: self.st.enter_context(nc.semaphore("sem_" + k))
                    for k in self.c}

    def sb(self, name, shape, dt, stack=None):
        return (stack or self.st).enter_context(
            self.nc.sbuf_tensor(name, list(shape), dt))

    def ps(self, name, shape, dt=F32, stack=None):
        return (stack or self.st).enter_context(
            self.nc.psum_tensor(name, list(shape), dt))

    def inc(self, instr, key, n=1):
        instr.then_inc(self.sem[key], n)
        self.c[key] += n
        return self.c[key]

    def dma(self, eng, dst, src, key):
        return self.inc(eng.dma_start(dst, src), key, 16)

    def dmat(self, eng, dst, src):
        return self.inc(eng.dma_start_transpose(dst, src), "h", 16)

    def edge(self, eng, instr, key="ve"):
        v = self.inc(instr, key)
        eng.wait_ge(self.sem[key], v)
        return v

    def wait(self, eng, **kv):
        for k, v in kv.items():
            if v and v > 0:
                eng.wait_ge(self.sem[k], v)

    @contextlib.contextmanager
    def step(self):
        blk = self.nc.Block()
        blk.__enter__()
        try:
            yield blk
        finally:
            blk.__exit__(None, None, None)


def build_program(n_chunks=16, dbg=False):
    TOK = CHK * n_chunks
    NTILE = n_chunks + 1
    wins = []
    t0 = 0
    while t0 < TOK:
        ww = min(512, TOK - t0)
        wins.append((t0, ww))
        t0 += ww
    NWIN = len(wins)
    OPB = 4 if n_chunks % 4 == 0 else 2
    n_b = n_chunks // OPB

    nc = bass.Bass("TRN2", target_bir_lowering=False,
                   detect_race_conditions=False)
    X = Ctx(nc)

    xh = nc.dram_tensor("xh", [NTILE * 128, D], F32, kind="ExternalInput")
    wiT = nc.dram_tensor("wiT", [D, PROJ], F32, kind="ExternalInput")
    woT = nc.dram_tensor("woT", [DIN, D], F32, kind="ExternalInput")
    wfT = nc.dram_tensor("wfT", [D, D], F16, kind="ExternalInput")
    bias_f = nc.dram_tensor("bias_f", [1, D], F16, kind="ExternalInput")
    convw = nc.dram_tensor("convw", [128, NB_HBC * KC], F32, kind="ExternalInput")
    convb = nc.dram_tensor("convb", [128, NB_HBC], F32, kind="ExternalInput")
    dtb = nc.dram_tensor("dtb", [128, 1], F32, kind="ExternalInput")
    alog = nc.dram_tensor("alog", [128, 1], F32, kind="ExternalInput")
    dpv = nc.dram_tensor("dpv", [128, 1], F32, kind="ExternalInput")
    nwv = nc.dram_tensor("nwv", [128, NKT], F32, kind="ExternalInput")
    gwv = nc.dram_tensor("gwv", [128, DIN // 128], F32, kind="ExternalInput")
    relG = nc.dram_tensor("relG", [128, 16], F32, kind="ExternalInput")
    relM = nc.dram_tensor("relM", [128, 4], F32, kind="ExternalInput")
    out = nc.dram_tensor("out", [TOK, D], F32, kind="ExternalOutput")

    sp_xn = nc.dram_tensor("sp_xn", [NTILE * 128, D], BF16)
    sp_hbc = nc.dram_tensor("sp_hbc", [CONV, TOK], BF16)
    sp_gate = nc.dram_tensor("sp_gate", [DIN, TOK], BF16)
    sp_acs = nc.dram_tensor("sp_acs", [128, TOK], F32)
    sp_y = nc.dram_tensor("sp_y", [TOK, DIN], BF16)
    sp_x1 = nc.dram_tensor("sp_x1", [TOK, D], F32)
    sp_x1n = nc.dram_tensor("sp_x1n", [TOK, D], BF16)
    sp_woT = nc.dram_tensor("sp_woT", [DIN, D], BF16)
    rel_in = nc.dram_tensor("rel_in", [128, DIN + 1], F32)
    if dbg:
        sp_dbg1 = nc.dram_tensor("sp_dbg1", [128, NKT * TOK], BF16)
        sp_dbg2 = nc.dram_tensor("sp_dbg2", [128, NKT * 128], BF16)
        sp_dbg3 = nc.dram_tensor("sp_dbg3", [128, DIN], BF16)
        sp_dbg4 = nc.dram_tensor("sp_dbg4", [128, 128], F32)
        sp_dbg5 = nc.dram_tensor("sp_dbg5", [128, 128], F32)
        sp_dbg6 = nc.dram_tensor("sp_dbg6", [128, 1024], BF16)
        sp_dbg7 = nc.dram_tensor("sp_dbg7", [128, DIN], F32)
        sp_dbg8 = nc.dram_tensor("sp_dbg8", [128, DIN], F32)
        sp_dbg9 = nc.dram_tensor("sp_dbg9", [128, H * CHK], BF16)
        sp_dbgA = nc.dram_tensor("sp_dbgA", [128, 512], F32)
        sp_dbgB = nc.dram_tensor("sp_dbgB", [128, 1024], BF16)
        sp_dbgC = nc.dram_tensor("sp_dbgC", [128, 2048], F32)
        sp_dbgD = nc.dram_tensor("sp_dbgD", [128, DIN], BF16)
    rel_out = nc.dram_tensor("rel_out", [4 * 128, DIN + 1], F32)

    idn_f32 = nc.inline_tensor(np.eye(128, dtype=np.float32), "idn_f32")
    ones_row = nc.inline_tensor(np.ones((1, 128), dtype=bfdt), "ones_row")
    triu_c = nc.inline_tensor(np.triu(np.ones((128, 128))).astype(bfdt),
                              "triu_c")
    ones_sq = nc.inline_tensor(np.ones((128, 128), dtype=np.float32),
                               "ones_sq")

    sb_convw = X.sb("c_convw", [128, NB_HBC * KC], F32)
    sb_convb = X.sb("c_convb", [128, NB_HBC], F32)
    sb_dtb = X.sb("c_dtb", [128, 1], F32)
    sb_negA = X.sb("c_negA", [128, 1], F32)
    sb_nw = X.sb("c_nw", [128, NKT], F32)
    sb_gw = X.sb("c_gw", [128, DIN // 128], F32)
    sb_idn = X.sb("c_idn", [128, 128], F32)
    sb_ones_row = X.sb("c_onesrow", [1, 128], BF16)
    sb_triu = X.sb("c_triu", [128, 128], BF16)
    sb_onesq = X.sb("c_onesq", [128, 128], F32)
    sb_eps = X.sb("c_eps", [128, 1], F32)
    sb_one = X.sb("c_one", [128, 1], F32)
    sb_relG = X.sb("c_relG", [128, 16], F32)
    sb_relM = X.sb("c_relM", [128, 4], F32)
    sb_stats_m = X.sb("c_stats_m", [128, NTILE], F32)
    sb_stats_r = X.sb("c_stats_r", [128, NTILE], F32)
    sb_stats_rp = X.sb("c_stats_rp", [128, NTILE], F32)
    sb_stats_nm = X.sb("c_stats_nm", [128, NTILE], F32)
    ab = X.st.enter_context(contextlib.ExitStack())
    sb_dtsp = X.sb("c_dtsp", [128, TOK], F32, ab)
    sb_acs = X.sb("c_acs", [128, TOK], F32, ab)
    sb_atot = X.sb("c_atot", [128, n_chunks], F32, ab)
    sb_asfx = X.sb("c_asfx", [128, n_chunks], F32, ab)
    sb_ea = X.sb("c_ea", [128, n_chunks], F32, ab)
    sb_atcl = X.sb("c_atcl", [128, n_chunks], F32, ab)
    sb_Sbf = X.sb("c_Sbf", [128, DIN], BF16, ab)
    sb_dprc = X.sb("c_dprc", [128, 128], F32, ab)

    with X.step() as blk:
        @blk.gpsimd
        def _(g):
            X.dma(g, sb_convw[:, :], convw[:, :], "g")
            X.dma(g, sb_convb[:, :], convb[:, :], "g")
            X.dma(g, sb_dtb[:, :], dtb[:, :], "g")
            X.dma(g, sb_negA[:, :], alog[:, :], "g")
            X.dma(g, sb_nw[:, :], nwv[:, :], "g")
            X.dma(g, sb_gw[:, :], gwv[:, :], "g")
            X.dma(g, sb_idn[:, :], idn_f32[:, :], "g")
            X.dma(g, sb_ones_row[:, :], ones_row[:, :], "g")
            X.dma(g, sb_triu[:, :], triu_c[:, :], "g")
            X.dma(g, sb_onesq[:, :], ones_sq[:, :], "g")
            X.dma(g, sb_relG[:, :], relG[:, :], "g")
            X.dma(g, sb_relM[:, :], relM[:, :], "g")
            for i in range(8):
                X.dma(g, sp_woT[i * 1024:(i + 1) * 1024, :],
                      woT[i * 1024:(i + 1) * 1024, :], "g")

        @blk.vector
        def _(ve):
            X.inc(ve.memset(sb_eps[:, :], EPS), "ve")
            X.inc(ve.memset(sb_one[:, :], 1.0), "ve")

        @blk.scalar
        def _(se):
            X.wait(se, g=X.c["g"])
            se.activation(sb_negA[:, :], sb_negA[:, :], AF.Exp)
            X.inc(se.activation(sb_negA[:, :], sb_negA[:, :], AF.Identity,
                                scale=-1.0), "se")
    m0 = dict(X.c)

    a0 = X.st.enter_context(contextlib.ExitStack())
    sb_dpcol = X.sb("t_dpcol", [128, 1], F32, a0)
    sb_dprow = X.sb("t_dprow", [1, 128], BF16, a0)
    ps_t0 = X.ps("t_ps0", [128, 512], F32, a0)
    with X.step() as blk:
        @blk.gpsimd
        def _(g):
            X.dma(g, sb_dpcol[:, :], dpv[:, :], "g")

        @blk.tensor
        def _(te):
            X.wait(te, g=X.c["g"])
            X.inc(te.transpose(ps_t0[0:1, 0:128], sb_dpcol[:, :],
                               sb_idn[:, :]), "pe")

        @blk.vector
        def _(ve):
            X.wait(ve, pe=X.c["pe"])
            X.inc(ve.tensor_copy(sb_dprow[:, :], ps_t0[0:1, 0:128]), "ve")

        @blk.tensor
        def _(te):
            X.wait(te, ve=X.c["ve"])
            X.inc(te.matmul(ps_t0[:, 0:128], sb_ones_row[0:1, :],
                            sb_dprow[0:1, :], start=True, stop=True), "pe")

        @blk.vector
        def _(ve):
            X.wait(ve, pe=X.c["pe"])
            X.inc(ve.tensor_copy(sb_dprc[:, :], ps_t0[:, 0:128]), "ve")
    a0.close()

    a1 = X.st.enter_context(contextlib.ExitStack())
    sb_xt = X.sb("a1_xt", [128, D], F32, a1)
    sb_xn = X.sb("a1_xn", [128, D], BF16, a1)
    sb_sq = X.sb("a1_sqbuf", [128, D], F32, a1)
    cl = {k: X.sb("a1_" + k, [128, 1], F32, a1)
          for k in ["sum", "sq", "m", "msq", "var", "r1", "t1", "t2",
                    "rs2", "rs12", "nmb"]}
    for t in range(NTILE):
        with X.step() as blk:
            @blk.gpsimd
            def _(g, t=t):
                X.wait(g, se=X.c["se"], h=X.c["h"])
                X.dma(g, sb_xt[:, :], xh[t * 128:(t + 1) * 128, :], "g")

            @blk.vector
            def _(ve, t=t):
                X.wait(ve, g=X.c["g"])
                X.inc(ve.tensor_reduce(cl["sum"][:, :], sb_xt[:, :], AXX,
                                       ALU.add), "ve")

            @blk.scalar
            def _(se, t=t):
                X.wait(se, g=X.c["g"])
                X.edge(se, se.activation(sb_sq[:, :], sb_xt[:, :],
                                         AF.Square,
                                         accum_out=cl["sq"][:, :]), "se")
                X.wait(se, ve=X.c["ve"])
                se.activation(cl["m"][:, :], cl["sum"][:, :], AF.Identity,
                              scale=1.0 / D)
                se.activation(cl["msq"][:, :], cl["sq"][:, :], AF.Identity,
                              scale=1.0 / D)
                X.edge(se, se.activation(cl["msq"][:, :], cl["msq"][:, :],
                                         AF.Identity), "se")
                se.activation(cl["t1"][:, :], cl["m"][:, :], AF.Identity,
                              scale=cl["m"][:, :])
                X.edge(se, se.activation(cl["t1"][:, :], cl["t1"][:, :],
                                         AF.Identity), "se")
                se.activation(cl["var"][:, :], cl["t1"][:, :], AF.Identity,
                              scale=-1.0, bias=cl["msq"][:, :])
                se.activation(cl["t1"][:, :], cl["var"][:, :], AF.Ln,
                              bias=sb_eps[:, :])
                X.edge(se, se.activation(cl["r1"][:, :], cl["t1"][:, :],
                                         AF.Exp, scale=-0.5), "se")
                se.activation(cl["t2"][:, :], cl["var"][:, :], AF.Identity,
                              scale=cl["r1"][:, :])
                X.edge(se, se.activation(cl["t2"][:, :], cl["t2"][:, :],
                                         AF.Identity, scale=cl["r1"][:, :]),
                       "se")
                se.activation(cl["t1"][:, :], cl["t2"][:, :], AF.Ln,
                              bias=sb_eps[:, :])
                X.edge(se, se.activation(cl["rs2"][:, :], cl["t1"][:, :],
                                         AF.Exp, scale=-0.5), "se")
                X.edge(se, se.activation(cl["rs12"][:, :], cl["r1"][:, :],
                                         AF.Identity, scale=cl["rs2"][:, :]),
                       "se")
                X.edge(se, se.activation(cl["nmb"][:, :], cl["m"][:, :],
                                         AF.Identity, scale=cl["rs12"][:, :]),
                       "se")
                X.edge(se, se.activation(cl["nmb"][:, :], cl["nmb"][:, :],
                                         AF.Identity, scale=-1.0), "se")
                se.copy(sb_stats_m[:, t:t + 1], cl["m"][:, :])
                se.copy(sb_stats_r[:, t:t + 1], cl["r1"][:, :])
                X.wait(se, h=X.c["h"])
                X.inc(se.activation(sb_xn[:, :], sb_xt[:, :], AF.Identity,
                                    bias=cl["nmb"][:, :],
                                    scale=cl["rs12"][:, :]), "se")

            @blk.sync
            def _(sy, t=t):
                X.wait(sy, se=X.c["se"])
                X.dma(sy, sp_xn[t * 128:(t + 1) * 128, :], sb_xn[:, :], "h")
    with X.step() as blk:
        @blk.vector
        def _(ve):
            X.wait(ve, se=X.c["se"])
            ve.tensor_scalar(sb_stats_rp[:, :], sb_stats_r[:, :], 1.0, None,
                             ALU.add)
            X.edge(ve, ve.tensor_tensor(sb_stats_nm[:, :], sb_stats_m[:, :],
                                        sb_stats_r[:, :], ALU.mult))
            X.inc(ve.tensor_scalar(sb_stats_nm[:, :], sb_stats_nm[:, :],
                                   -1.0, None, ALU.mult), "ve")
    a1.close()
    m_a1 = dict(X.c)

    a3 = X.st.enter_context(contextlib.ExitStack())
    sb_hT = X.sb("a_hT", [128, NKT * TOK], BF16, a3)
    sb_hTh = X.sb("a_hTh", [128, NKT * 128], BF16, a3)
    sb_ws = [X.sb(f"a_ws{i}", [128, NKT * 128], BF16, a3) for i in range(2)]
    sb_stg = [X.sb(f"a_stg{i}", [128, 512], BF16, a3) for i in range(4)]
    sb_cacc = X.sb("a_cacc", [128, 512], F32, a3)
    sb_halo = X.sb("a_halo", [128, NB_HBC * 3], F32, a3)
    sb_sp1 = X.sb("a_sp1", [128, 512], F32, a3)
    ps_m = [X.ps(f"a_pm{i}", [128, 512], F32, a3) for i in range(4)]
    ps_r = [X.ps(f"a_pr{i}", [128, 512], F32, a3) for i in range(2)]

    with X.step() as blk:
        @blk.sync
        def _(sy):
            X.wait(sy, h=m_a1["h"])
            for kt in range(NKT):
                X.dmat(sy, sb_hTh[:, kt * 128:(kt + 1) * 128],
                       sp_xn.ap()[0:128, kt * 128:(kt + 1) * 128])
                X.dmat(sy, sb_hT[:, kt * TOK:(kt + 1) * TOK],
                       sp_xn.ap()[128:128 + TOK, kt * 128:(kt + 1) * 128])
    m_hTl = dict(X.c)
    if dbg:
        with X.step() as blk:
            @blk.sync
            def _(sy):
                X.dma(sy, sp_dbg1[:, :], sb_hT[:, :], "h")
    with X.step() as blk:
        @blk.vector
        def _(ve):
            X.wait(ve, h=m_hTl["h"] if not dbg else X.c["h"])
            for kt in range(NKT):
                ve.tensor_scalar(sb_hT[:, kt * TOK:(kt + 1) * TOK],
                                 sb_hT[:, kt * TOK:(kt + 1) * TOK],
                                 sb_nw[:, kt:kt + 1], None, ALU.mult)
                X.inc(ve.tensor_scalar(
                    sb_hTh[:, kt * 128:(kt + 1) * 128],
                    sb_hTh[:, kt * 128:(kt + 1) * 128],
                    sb_nw[:, kt:kt + 1], None, ALU.mult), "ve")
    m_a2 = dict(X.c)

    slab_ld = {}
    mm_done = {}
    runt_done = {}
    ev_done = {}
    sp_done = {}

    def wait_ev(eng, key):
        v = ev_done.get(key)
        if v:
            X.wait(eng, **{v[0]: v[1]})

    for bi in range(NBLK):
        is_dt = bi == 0
        is_hbc = 1 <= bi <= NB_HBC
        ci_hbc = bi - 1
        gi = bi - 1 - NB_HBC
        with X.step() as blk:
            @blk.gpsimd
            def _(g, bi=bi):
                if bi >= 2:
                    X.wait(g, pe=mm_done[(bi - 2, NWIN - 1)])
                c0 = bi * 128
                slab_ld[bi] = X.dma(
                    g, sb_ws[bi % 2].ap().rearrange("p (kt c) -> p kt c",
                                                    kt=NKT),
                    wiT.ap()[:, c0:c0 + 128].rearrange("(kt p) c -> p kt c",
                                                       p=128), "g")

            if dbg and bi == 1:
                @blk.sync
                def _(sy, bi=bi):
                    X.wait(sy, g=slab_ld[bi])
                    X.dma(sy, sp_dbg2[:, :], sb_ws[bi % 2][:, :], "h")

            @blk.tensor
            def _(te, bi=bi, is_hbc=is_hbc):
                X.wait(te, g=slab_ld[bi])
                if dbg and bi == 1:
                    X.wait(te, h=X.c["h"])
                if bi < 2:
                    X.wait(te, ve=m_a2["ve"])
                slab = sb_ws[bi % 2]
                if is_hbc:
                    if bi >= 3 and (bi - 2, 0) in ev_done:
                        wait_ev(te, (bi - 2, 0))
                    for kt in range(NKT):
                        ins = te.matmul(
                            ps_r[bi % 2][:, 0:3],
                            slab[:, kt * 128:(kt + 1) * 128],
                            sb_hTh[:, kt * 128 + 125:kt * 128 + 128],
                            start=(kt == 0), stop=(kt == NKT - 1))
                    runt_done[bi] = X.inc(ins, "pe")
                for w, (ts, ww) in enumerate(wins):
                    gwi = bi * NWIN + w
                    if gwi >= 4:
                        pbi, pw = divmod(gwi - 4, NWIN)
                        wait_ev(te, (pbi, pw))
                        if pw + 1 < NWIN:
                            wait_ev(te, (pbi, pw + 1))
                    for kt in range(NKT):
                        ins = te.matmul(
                            ps_m[gwi % 4][:, 0:ww],
                            slab[:, kt * 128:(kt + 1) * 128],
                            sb_hT[:, kt * TOK + ts:kt * TOK + ts + ww],
                            start=(kt == 0), stop=(kt == NKT - 1))
                    mm_done[(bi, w)] = X.inc(ins, "pe")

            if is_dt:
                @blk.scalar
                def _(se, bi=bi):
                    for w, (ts, ww) in enumerate(wins):
                        X.wait(se, pe=mm_done[(bi, w)])
                        se.activation(sb_sp1[:, 0:ww],
                                      ps_m[(bi * NWIN + w) % 4][:, 0:ww],
                                      AF.Exp, bias=sb_dtb[:, :])
                        v = X.inc(se.activation(sb_dtsp[:, ts:ts + ww],
                                                sb_sp1[:, 0:ww], AF.Ln,
                                                bias=sb_one[:, :]), "se")
                        ev_done[(bi, w)] = ("se", v)
            elif is_hbc:
                @blk.vector
                def _(ve, bi=bi, ci_hbc=ci_hbc):
                    X.wait(ve, pe=runt_done[bi])
                    X.inc(ve.tensor_copy(
                        sb_halo[:, ci_hbc * 3:ci_hbc * 3 + 3],
                        ps_r[bi % 2][:, 0:3]), "ve")
                for w, (ts, ww) in enumerate(wins):
                    @blk.vector
                    def _(ve, bi=bi, ci_hbc=ci_hbc, w=w, ts=ts, ww=ww):
                        X.wait(ve, pe=mm_done[(bi, w)])
                        X.wait(ve, se=X.c["se"])
                        pm = ps_m[(bi * NWIN + w) % 4]
                        wci = ci_hbc * KC
                        ve.tensor_scalar(sb_cacc[:, 0:ww], pm[:, 0:ww],
                                         sb_convw[:, wci + 3:wci + 4], None,
                                         ALU.mult)
                        for k in (1, 2, 3):
                            wc = sb_convw[:, wci + 3 - k:wci + 4 - k]
                            ve.scalar_tensor_tensor(
                                sb_cacc[:, k:ww], pm[:, 0:ww - k], wc,
                                sb_cacc[:, k:ww], ALU.mult, ALU.add)
                        last = None
                        for k in (1, 2, 3):
                            wc = sb_convw[:, wci + 3 - k:wci + 4 - k]
                            if w == 0:
                                prev = sb_halo[:, ci_hbc * 3 + 3 - k:
                                               ci_hbc * 3 + 3]
                            else:
                                ppm = ps_m[(bi * NWIN + w - 1) % 4]
                                pww = wins[w - 1][1]
                                prev = ppm[:, pww - k:pww]
                            if last is not None:
                                X.edge(ve, last)
                            last = ve.scalar_tensor_tensor(
                                sb_cacc[:, 0:k], prev, wc,
                                sb_cacc[:, 0:k], ALU.mult, ALU.add)
                        v = X.inc(last, "ve")
                        ev_done[(bi, w)] = ("ve", v)

                    @blk.scalar
                    def _(se, bi=bi, ci_hbc=ci_hbc, w=w, ww=ww):
                        wait_ev(se, (bi, w))
                        psp = sp_done.get((bi - 1, w))
                        if psp:
                            X.wait(se, h=psp)
                        X.inc(se.activation(
                            sb_stg[w % 4][:, 0:ww], sb_cacc[:, 0:ww],
                            AF.Silu, bias=sb_convb[:, ci_hbc:ci_hbc + 1]),
                            "se")

                    @blk.sync
                    def _(sy, bi=bi, ci_hbc=ci_hbc, w=w, ts=ts, ww=ww):
                        X.wait(sy, se=X.c["se"])
                        sp_done[(bi, w)] = X.dma(
                            sy, sp_hbc[ci_hbc * 128:(ci_hbc + 1) * 128,
                                       ts:ts + ww],
                            sb_stg[w % 4][:, 0:ww], "h")
            else:
                for w, (ts, ww) in enumerate(wins):
                    @blk.scalar
                    def _(se, bi=bi, w=w, ww=ww):
                        X.wait(se, pe=mm_done[(bi, w)])
                        psp = sp_done.get((bi - 1, w))
                        if psp:
                            X.wait(se, h=psp)
                        v = X.inc(se.activation(
                            sb_stg[w % 4][:, 0:ww],
                            ps_m[(bi * NWIN + w) % 4][:, 0:ww], AF.Silu),
                            "se")
                        ev_done[(bi, w)] = ("se", v)

                    @blk.sync
                    def _(sy, bi=bi, gi=gi, w=w, ts=ts, ww=ww):
                        X.wait(sy, se=X.c["se"])
                        sp_done[(bi, w)] = X.dma(
                            sy, sp_gate[gi * 128:(gi + 1) * 128, ts:ts + ww],
                            sb_stg[w % 4][:, 0:ww], "h")
    m_a3 = dict(X.c)

    with X.step() as blk:
        @blk.vector
        def _(ve):
            X.wait(ve, se=m_a3["se"])
            X.inc(ve.tensor_scalar(sb_acs[:, :], sb_dtsp[:, :],
                                   sb_negA[:, :], None, ALU.mult), "ve")
            for ci in range(n_chunks):
                s = ci * CHK
                X.inc(ve.tensor_tensor_scan(
                    sb_acs[:, s:s + CHK], sb_onesq[:, :CHK],
                    sb_acs[:, s:s + CHK], 0.0, ALU.mult, ALU.add), "ve")

        @blk.scalar
        def _(se):
            X.wait(se, ve=X.c["ve"])
            for ci in range(n_chunks):
                se.copy(sb_atot[:, ci:ci + 1],
                        sb_acs[:, ci * CHK + CHK - 1:ci * CHK + CHK])
            last = se.copy(sb_asfx[:, n_chunks - 1:n_chunks],
                           sb_atot[:, n_chunks - 1:n_chunks])
            for ci in range(n_chunks - 2, -1, -1):
                X.edge(se, last, "se")
                last = se.activation(sb_asfx[:, ci:ci + 1],
                                     sb_atot[:, ci:ci + 1],
                                     AF.Identity,
                                     bias=sb_asfx[:, ci + 1:ci + 2])
            X.inc(last, "se")

        @blk.vector
        def _(ve):
            X.wait(ve, se=X.c["se"])
            X.inc(ve.tensor_scalar(sb_atcl[:, :], sb_atot[:, :], -85.0,
                                   None, ALU.max), "ve")

        @blk.scalar
        def _(se):
            X.wait(se, ve=X.c["ve"])
            X.inc(se.activation(sb_ea[:, :], sb_atcl[:, :], AF.Exp), "se")

        @blk.sync
        def _(sy):
            X.wait(sy, ve=X.c["ve"])
            X.dma(sy, sp_acs[:, :], sb_acs[:, :], "h")
    m_dt = dict(X.c)
    a3.close()

    at = X.st.enter_context(contextlib.ExitStack())
    sb_Sl = [X.sb(f"t_Sl{i}", [128, DIN], F32, at) for i in range(2)]
    sb_hidA = X.sb("t_hidA", [128, DIN], BF16, at)
    sb_BnA = X.sb("t_BnA", [128, G * 128], BF16, at)
    sb_hd1A = X.sb("t_hd1A", [128, 1024], BF16, at)
    sb_dwA = X.sb("t_dwA", [128, 128], F32, at)
    sb_dwAT = X.sb("t_dwAT", [128, 128], F32, at)
    sb_Sin = X.sb("t_Sin", [128, DIN], F32, at)
    sb_rtmp = X.sb("t_rtmp", [128, DIN], F32, at)
    sb_atg = X.sb("t_atg", [128, 4], F32, at)
    sb_cof = X.sb("t_cof", [128, 1], F32, at)
    sb_cofr = X.sb("t_cofr", [1, 128], BF16, at)
    sb_CFr = X.sb("t_CFr", [128, 128], F32, at)
    ps_w = [X.ps(f"t_pw{i}", [128, 1024], F32, at) for i in range(2)]
    ps_t = X.ps("t_pt", [128, 512], F32, at)

    with X.step() as blk:
        @blk.vector
        def _(ve):
            X.inc(ve.memset(sb_Sl[n_chunks % 2][:, :], 0.0), "ve")
            X.inc(ve.memset(sb_Sin[:, :], 0.0), "ve")

    for ci in range(n_chunks):
        c0 = ci * CHK
        with X.step() as blk:
            @blk.sync
            def _(sy, ci=ci, c0=c0):
                X.wait(sy, h=m_a3["h"])
                if ci >= 1:
                    X.wait(sy, pe=X.c["pe"])
                X.dmat(sy, sb_hidA[:, :], sp_hbc.ap()[0:DIN, c0:c0 + CHK])
                X.dmat(sy, sb_BnA[:, :],
                       sp_hbc.ap()[DIN:DIN + 1024, c0:c0 + CHK])

            @blk.vector
            def _(ve, ci=ci, c0=c0):
                X.wait(ve, se=m_dt["se"])
                ve.tensor_scalar(sb_dwA[:, :], sb_acs[:, c0:c0 + CHK],
                                 sb_asfx[:, ci:ci + 1], None, ALU.subtract)
                X.inc(ve.tensor_scalar(sb_dwA[:, :], sb_dwA[:, :], -1.0,
                                       -85.0, ALU.mult, ALU.max), "ve")

            @blk.scalar
            def _(se, ci=ci):
                X.wait(se, ve=X.c["ve"])
                X.inc(se.activation(sb_dwA[:, :], sb_dwA[:, :], AF.Exp), "se")

            @blk.vector
            def _(ve, ci=ci, c0=c0):
                X.wait(ve, se=X.c["se"])
                X.inc(ve.tensor_tensor(sb_dwA[:, :], sb_dwA[:, :],
                                       sb_dtsp[:, c0:c0 + CHK], ALU.mult),
                      "ve")

            @blk.tensor
            def _(te, ci=ci):
                X.wait(te, ve=X.c["ve"])
                if ci >= 1:
                    X.wait(te, ve=X.c["ve"])
                X.inc(te.transpose(ps_t[:, 0:128], sb_dwA[:, :],
                                   sb_idn[:, :]), "pe")

            @blk.vector
            def _(ve, ci=ci):
                X.wait(ve, pe=X.c["pe"])
                X.inc(ve.tensor_copy(sb_dwAT[:, :], ps_t[:, 0:128]), "ve")

            if dbg and ci == 0:
                @blk.sync
                def _(sy):
                    X.wait(sy, h=X.c["h"], ve=X.c["ve"], se=X.c["se"])
                    X.dma(sy, sp_dbg3.ap(), sb_hidA[:, :], "h")
                    X.dma(sy, sp_dbg4.ap(), sb_dwA[:, :], "h")
                    X.dma(sy, sp_dbg5.ap(), sb_dwAT[:, :], "h")

            for wv in range(8):
                @blk.vector
                def _(ve, ci=ci, wv=wv):
                    X.wait(ve, h=X.c["h"])
                    if ci >= 1 or wv >= 1:
                        X.wait(ve, pe=X.c["pe"])
                    last = None
                    for hh in range(16):
                        ha = wv * 16 + hh
                        last = ve.tensor_scalar(
                            sb_hd1A[:, hh * 64:(hh + 1) * 64],
                            sb_hidA[:, ha * 64:(ha + 1) * 64],
                            sb_dwAT[:, ha:ha + 1], None, ALU.mult)
                    X.inc(last, "ve")

                @blk.tensor
                def _(te, ci=ci, wv=wv):
                    X.wait(te, ve=X.c["ve"])
                    last = None
                    for hh in range(16):
                        ha = wv * 16 + hh
                        g = ha // 16
                        last = te.matmul(
                            ps_w[wv % 2][:, hh * 64:(hh + 1) * 64],
                            sb_BnA[:, g * 128:(g + 1) * 128],
                            sb_hd1A[:, hh * 64:(hh + 1) * 64],
                            start=(hh % 8 == 0), stop=True,
                            skip_group_check=True)
                    X.inc(last, "pe")

                @blk.vector
                def _(ve, ci=ci, wv=wv):
                    X.wait(ve, pe=X.c["pe"])
                    sl = slice(wv * 1024, (wv + 1) * 1024)
                    X.inc(ve.tensor_tensor(
                        sb_Sl[(ci + 1) % 2][:, sl],
                        sb_Sl[ci % 2][:, sl], ps_w[wv % 2][:, :], ALU.add),
                        "ve")

                if dbg and ci == 0 and wv == 0:
                    @blk.sync
                    def _(sy):
                        X.wait(sy, ve=X.c["ve"])
                        X.dma(sy, sp_dbg6.ap(), sb_hd1A[:, :], "h")
            if dbg and ci == 0:
                @blk.sync
                def _(sy, ci=ci):
                    X.wait(sy, ve=X.c["ve"])
                    X.dma(sy, sp_dbg7.ap(), sb_Sl[(ci + 1) % 2][:, :], "h")
    m_at = dict(X.c)

    with X.step() as blk:
        @blk.sync
        def _(sy):
            X.wait(sy, ve=m_at["ve"])
            X.dma(sy, rel_in[:, 0:DIN], sb_Sl[n_chunks % 2][:, :], "h")
            with nc.allow_non_contiguous_dma("tiny decay column"):
                X.dma(sy, rel_in[:, DIN:DIN + 1], sb_asfx[:, 0:1], "h")

        @blk.gpsimd
        def _(g):
            X.wait(g, h=X.c["h"])
            X.inc(g.collective_compute(
                "AllGather", ALU.bypass,
                replica_groups=[[0, 1, 2, 3], [4, 5, 6, 7]],
                ins=[rel_in.ap().opt()],
                outs=[rel_out.ap().opt()]), "cc")
            g.wait_ge(X.sem["cc"], X.c["cc"])
            with nc.allow_non_contiguous_dma("tiny decay columns"):
                for r in range(4):
                    X.dma(g, sb_atg[:, r:r + 1],
                          bass.AP(rel_out, r * 128 * (DIN + 1) + DIN,
                                  [[DIN + 1, 128], [1, 1]]), "g")
    m_rel = dict(X.c)

    for r in range(4):
        with X.step() as blk:
            @blk.gpsimd
            def _(g, r=r):
                X.wait(g, g=m_rel["g"])
                if r >= 1:
                    X.wait(g, ve=X.c["ve"])
                X.dma(g, sb_rtmp[:, :],
                      rel_out.ap()[r * 128:(r + 1) * 128, 0:DIN], "g")

            @blk.vector
            def _(ve, r=r):
                X.wait(ve, g=X.c["g"])
                last = ve.tensor_scalar(sb_cof[:, :], sb_atg[:, 0:1],
                                        sb_relG[:, r * 4:r * 4 + 1], None,
                                        ALU.mult)
                for mm2 in range(1, 4):
                    X.edge(ve, last)
                    last = ve.scalar_tensor_tensor(
                        sb_cof[:, :], sb_atg[:, mm2:mm2 + 1],
                        sb_relG[:, r * 4 + mm2:r * 4 + mm2 + 1],
                        sb_cof[:, :], ALU.mult, ALU.add)
                X.edge(ve, last)
                X.inc(ve.tensor_scalar(sb_cof[:, :], sb_cof[:, :], -85.0,
                                       None, ALU.max), "ve")

            @blk.scalar
            def _(se, r=r):
                X.wait(se, ve=X.c["ve"])
                X.inc(se.activation(sb_cof[:, :], sb_cof[:, :], AF.Exp), "se")

            @blk.vector
            def _(ve, r=r):
                X.wait(ve, se=X.c["se"])
                X.inc(ve.tensor_tensor(sb_cof[:, :], sb_cof[:, :],
                                       sb_relM[:, r:r + 1], ALU.mult), "ve")

            @blk.tensor
            def _(te, r=r):
                X.wait(te, ve=X.c["ve"])
                X.inc(te.transpose(ps_t[0:1, 0:128], sb_cof[:, :],
                                   sb_idn[:, :]), "pe")

            @blk.vector
            def _(ve, r=r):
                X.wait(ve, pe=X.c["pe"])
                X.inc(ve.tensor_copy(sb_cofr[:, :], ps_t[0:1, 0:128]), "ve")

            @blk.tensor
            def _(te, r=r):
                X.wait(te, ve=X.c["ve"])
                X.inc(te.matmul(ps_t[:, 0:128], sb_ones_row[0:1, :],
                                sb_cofr[0:1, :], start=True, stop=True),
                      "pe")

            @blk.vector
            def _(ve, r=r):
                X.wait(ve, pe=X.c["pe"])
                X.inc(ve.tensor_copy(sb_CFr[:, :], ps_t[:, 0:128]), "ve")
                last = None
                for ha in range(H):
                    last = ve.scalar_tensor_tensor(
                        sb_Sin[:, ha * 64:(ha + 1) * 64],
                        sb_rtmp[:, ha * 64:(ha + 1) * 64],
                        sb_CFr[:, ha:ha + 1],
                        sb_Sin[:, ha * 64:(ha + 1) * 64], ALU.mult, ALU.add)
                X.inc(last, "ve")
    with X.step() as blk:
        @blk.vector
        def _(ve):
            X.wait(ve, ve=X.c["ve"])
            X.inc(ve.tensor_copy(sb_Sbf[:, :], sb_Sin[:, :]), "ve")

        if dbg:
            @blk.sync
            def _(sy):
                X.wait(sy, ve=X.c["ve"])
                X.dma(sy, sp_dbg8.ap(), sb_Sin[:, :], "h")
    m_sin = dict(X.c)
    at.close()

    bb = X.st.enter_context(contextlib.ExitStack())
    sb_mt = X.sb("b_mt", [128, max(H * CHK, (DIN // 128) * OPB * CHK)],
                 BF16, bb)
    sb_hid = X.sb("b_hid", [128, DIN], BF16, bb)
    sb_sgh = X.sb("b_sgh", [128, 4096], BF16, bb)
    sb_Bt = X.sb("b_Bt", [128, 1024], BF16, bb)
    sb_Ct = X.sb("b_Ct", [128, 1024], BF16, bb)
    sb_Bn = X.sb("b_Bn", [128, 1024], BF16, bb)
    sb_rr = [X.sb(f"b_rr{i}", [128, 2048], F32, bb) for i in range(2)]
    sb_t1 = X.sb("b_t1", [128, 2048], F32, bb)
    sb_GT = X.sb("b_GT", [128, 1024], BF16, bb)
    sb_alT = X.sb("b_alT", [128, 128], F32, bb)
    sb_e1T = X.sb("b_e1T", [128, 128], F32, bb)
    sb_e1c = X.sb("b_e1c", [128, 128], F32, bb)
    sb_dwT = X.sb("b_dwT", [128, 128], F32, bb)
    sb_EAr = X.sb("b_EAr", [128, 128], F32, bb)
    sb_erow = X.sb("b_erow", [1, 128], BF16, bb)
    sb_ltil = X.sb("b_ltil", [128, 128], F32, bb)
    sb_dtil = X.sb("b_dtil", [128, 128], F32, bb)
    sb_hd1 = X.sb("b_hd1", [128, 1024], BF16, bb)
    sb_yh = X.sb("b_yh", [128, 4096], F32, bb)
    sb_qc = X.sb("b_qc", [128, 1], F32, bb)
    sb_sc = X.sb("b_sc", [128, 1], F32, bb)
    sb_wos = [X.sb(f"b_wos{i}", [128, 2048], BF16, bb) for i in range(2)]
    sb_xsl = [X.sb(f"b_xsl{i}", [128, 512], F32, bb) for i in range(4)]
    sb_x1s = [X.sb(f"b_x1s{i}", [128, 512], F32, bb) for i in range(4)]
    ps_b = [X.ps(f"b_ps{i}", [128, 512], F32, bb) for i in range(8)]

    scan_done = m_sin["ve"]
    yT_W = 64 * OPB * CHK

    for ci in range(n_chunks):
        c0 = ci * CHK
        with X.step() as blk:
            @blk.sync
            def _(sy, ci=ci, c0=c0):
                X.wait(sy, pe=X.c["pe"], ve=X.c["ve"])
                X.dmat(sy, sb_hid[:, :], sp_hbc.ap()[0:DIN, c0:c0 + CHK])
                X.dmat(sy, sb_Bn[:, :],
                       sp_hbc.ap()[DIN:DIN + 1024, c0:c0 + CHK])
                X.dmat(sy, sb_sgh[:, :], sp_gate.ap()[0:4096, c0:c0 + CHK])

            @blk.gpsimd
            def _(g, ci=ci, c0=c0):
                X.wait(g, pe=X.c["pe"], h=m_a3["h"])
                X.dma(g, sb_Bt.ap().rearrange("p (gr t) -> p gr t", gr=8),
                      sp_hbc.ap()[DIN:DIN + 1024, c0:c0 + CHK]
                      .rearrange("(gr p) t -> p gr t", p=128), "g")
                X.dma(g, sb_Ct.ap().rearrange("p (gr t) -> p gr t", gr=8),
                      sp_hbc.ap()[DIN + 1024:DIN + 2048, c0:c0 + CHK]
                      .rearrange("(gr p) t -> p gr t", p=128), "g")
            m_ld = dict(X.c)

            @blk.scalar
            def _(se, ci=ci, c0=c0):
                X.wait(se, ve=X.c["ve"], pe=X.c["pe"])
                X.inc(se.activation(sb_ltil[:, :], sb_dtsp[:, c0:c0 + CHK],
                                    AF.Ln), "se")

            @blk.vector
            def _(ve, ci=ci, c0=c0):
                X.wait(ve, se=X.c["se"])
                ve.tensor_tensor(sb_ltil[:, :], sb_acs[:, c0:c0 + CHK],
                                 sb_ltil[:, :], ALU.subtract)
                X.edge(ve, ve.tensor_scalar(sb_dtil[:, :],
                                            sb_acs[:, c0:c0 + CHK],
                                            sb_atot[:, ci:ci + 1], None,
                                            ALU.subtract))
                X.inc(ve.tensor_scalar(sb_dtil[:, :], sb_dtil[:, :], -1.0,
                                       -85.0, ALU.mult, ALU.max), "ve")

            @blk.scalar
            def _(se, ci=ci):
                X.wait(se, ve=X.c["ve"])
                X.inc(se.activation(sb_dtil[:, :], sb_dtil[:, :], AF.Exp),
                      "se")

            @blk.vector
            def _(ve, ci=ci, c0=c0):
                X.wait(ve, se=X.c["se"])
                X.inc(ve.tensor_tensor(sb_dtil[:, :], sb_dtil[:, :],
                                       sb_dtsp[:, c0:c0 + CHK], ALU.mult),
                      "ve")

            @blk.tensor
            def _(te, ci=ci, c0=c0):
                X.wait(te, ve=X.c["ve"])
                te.transpose(ps_b[4][:, 0:128], sb_ltil[:, :], sb_idn[:, :])
                te.transpose(ps_b[4][:, 128:256], sb_dtil[:, :], sb_idn[:, :])
                te.transpose(ps_b[5][:, 0:128],
                             bass.AP(sb_acs, c0, [[TOK, 128], [1, CHK]]),
                             sb_idn[:, :])
                X.inc(te.transpose(ps_b[5][0:1, 128:256],
                                   sb_ea[:, ci:ci + 1], sb_idn[:, :]), "pe")

            @blk.vector
            def _(ve, ci=ci):
                X.wait(ve, pe=X.c["pe"])
                ve.tensor_copy(sb_alT[:, :], ps_b[4][:, 0:128])
                ve.tensor_copy(sb_dwT[:, :], ps_b[4][:, 128:256])
                X.inc(ve.tensor_copy(sb_erow[:, :], ps_b[5][0:1, 128:256]),
                      "ve")

            @blk.vector
            def _(ve, ci=ci):
                X.wait(ve, pe=X.c["pe"])
                X.inc(ve.tensor_scalar(sb_e1c[:, :], ps_b[5][:, 0:128],
                                       -85.0, None, ALU.max), "ve")

            @blk.scalar
            def _(se, ci=ci):
                X.wait(se, ve=X.c["ve"])
                X.inc(se.activation(sb_e1T[:, :], sb_e1c[:, :], AF.Exp),
                      "se")

            @blk.tensor
            def _(te, ci=ci):
                X.wait(te, ve=X.c["ve"])
                X.inc(te.matmul(ps_b[6][:, 0:128], sb_ones_row[0:1, :],
                                sb_erow[0:1, :], start=True, stop=True),
                      "pe")

            @blk.vector
            def _(ve, ci=ci):
                X.wait(ve, pe=X.c["pe"])
                X.inc(ve.tensor_copy(sb_EAr[:, :], ps_b[6][:, 0:128]), "ve")
            m_ear = dict(X.c)

            @blk.tensor
            def _(te, ci=ci):
                X.wait(te, g=m_ld["g"], ve=m_ear["ve"])
                last = None
                for g8 in range(8):
                    last = te.matmul(
                        ps_b[6 + g8 // 4][:, (g8 % 4) * 128:(g8 % 4 + 1) * 128],
                        sb_Bt[:, g8 * 128:(g8 + 1) * 128],
                        sb_Ct[:, g8 * 128:(g8 + 1) * 128],
                        start=(g8 % 4 == 0), stop=True,
                        skip_group_check=True)
                X.inc(last, "pe")

            @blk.vector
            def _(ve, ci=ci):
                X.wait(ve, pe=X.c["pe"])
                last = None
                for g8 in range(8):
                    last = ve.tensor_tensor(
                        sb_GT[:, g8 * 128:(g8 + 1) * 128],
                        ps_b[6 + g8 // 4][:, (g8 % 4) * 128:(g8 % 4 + 1) * 128],
                        sb_triu[:, :], ALU.mult)
                X.inc(last, "ve")
        m_gt = dict(X.c)

        ts_done = {}
        se_done = {}
        for e8 in range(8):
            with X.step() as blk:
                @blk.gpsimd
                def _(g, ci=ci, c0=c0, e8=e8):
                    X.wait(g, h=m_dt["h"])
                    if e8 >= 2:
                        X.wait(g, ve=ts_done[e8 - 2])
                    elif ci > 0:
                        X.wait(g, ve=m_gt["ve"])
                    X.dma(g, sb_rr[e8 % 2][:, :],
                          bass.AP(sp_acs, (e8 * 16) * TOK + c0,
                                  [[0, 128], [TOK, 16], [1, CHK]]), "g")

                @blk.vector
                def _(ve, ci=ci, e8=e8):
                    X.wait(ve, g=X.c["g"], h=m_dt["h"])
                    if e8 >= 1:
                        X.wait(ve, se=se_done[e8 - 1])
                    last = None
                    for hh in range(16):
                        ha = e8 * 16 + hh
                        last = ve.tensor_scalar(
                            sb_t1[:, hh * CHK:(hh + 1) * CHK],
                            sb_rr[e8 % 2][:, hh * CHK:(hh + 1) * CHK],
                            sb_alT[:, ha:ha + 1], 30.0,
                            ALU.subtract, ALU.min)
                    X.edge(ve, last)
                    last = ve.tensor_scalar(sb_t1[:, :], sb_t1[:, :], -85.0,
                                            None, ALU.max)
                    ts_done[e8] = X.inc(last, "ve")

                @blk.scalar
                def _(se, ci=ci, e8=e8):
                    X.wait(se, ve=ts_done[e8])
                    se_done[e8] = X.inc(se.activation(
                        sb_mt[:, e8 * 2048:(e8 + 1) * 2048], sb_t1[:, :],
                        AF.Exp), "se")

                @blk.vector
                def _(ve, ci=ci, e8=e8):
                    X.wait(ve, se=se_done[e8])
                    gsl = bass.AP(sb_GT, e8 * 128,
                                  [[1024, 128], [0, 16], [1, 128]])
                    X.inc(ve.tensor_tensor(
                        sb_mt[:, e8 * 2048:(e8 + 1) * 2048],
                        sb_mt[:, e8 * 2048:(e8 + 1) * 2048], gsl, ALU.mult),
                        "ve")
        m_mt = dict(X.c)

        if dbg and ci == n_chunks - 1:
            with X.step() as blk:
                @blk.sync
                def _(sy):
                    X.wait(sy, ve=m_mt["ve"], se=m_mt["se"], g=m_mt["g"])
                    X.dma(sy, sp_dbg9.ap(), sb_mt[:, 0:H * CHK], "h")
                    X.dma(sy, sp_dbgA.ap()[:, 0:128], sb_alT[:, :], "h")
                    X.dma(sy, sp_dbgA.ap()[:, 128:256], sb_e1T[:, :], "h")
                    X.dma(sy, sp_dbgA.ap()[:, 256:384], sb_dwT[:, :], "h")
                    X.dma(sy, sp_dbgA.ap()[:, 384:512], sb_EAr[:, :], "h")
                    X.dma(sy, sp_dbgB.ap(), sb_GT[:, :], "h")
                    X.dma(sy, sp_dbgC.ap(), sb_rr[0][:, :], "h")
                    X.dma(sy, sp_dbgD.ap(), sb_Sbf[:, :], "h")
            dbg_tap_h = X.c["h"]

        ev_y = {}
        half_spill = {}
        for wv in range(16):
            with X.step() as blk:
                @blk.tensor
                def _(te, ci=ci, wv=wv):
                    if wv == 0:
                        X.wait(te, ve=m_mt["ve"], h=m_ld["h"],
                               se=m_mt["se"])
                        X.wait(te, ve=scan_done)
                        if dbg and ci == n_chunks - 1:
                            X.wait(te, h=dbg_tap_h)
                    if wv >= 2:
                        X.wait(te, ve=ev_y[wv - 2])
                    last = None
                    for hh in range(8):
                        ha = wv * 8 + hh
                        te.matmul(ps_b[wv % 2][:, hh * 64:(hh + 1) * 64],
                                  sb_mt[:, ha * 128:(ha + 1) * 128],
                                  sb_hid[:, ha * 64:(ha + 1) * 64],
                                  start=(hh == 0), stop=True,
                                  skip_group_check=True)
                        last = te.matmul(
                            ps_b[2 + wv % 2][:, hh * 64:(hh + 1) * 64],
                            sb_Ct[:, (ha // 16) * 128:(ha // 16 + 1) * 128],
                            sb_Sbf[:, ha * 64:(ha + 1) * 64],
                            start=(hh == 0), stop=True,
                            skip_group_check=True)
                    X.inc(last, "pe")

                @blk.vector
                def _(ve, ci=ci, wv=wv):
                    X.wait(ve, pe=X.c["pe"])
                    if wv == 8:
                        X.wait(ve, g=half_spill[0])
                    ysl = slice((wv % 8) * 512, (wv % 8 + 1) * 512)
                    last = None
                    for hh in range(8):
                        ha = wv * 8 + hh
                        cs = (wv % 8) * 512 + hh * 64
                        last = ve.tensor_scalar(
                            sb_yh[:, cs:cs + 64],
                            ps_b[2 + wv % 2][:, hh * 64:(hh + 1) * 64],
                            sb_e1T[:, ha:ha + 1], None, ALU.mult)
                    X.edge(ve, last)
                    X.edge(ve, ve.tensor_tensor(sb_yh[:, ysl], sb_yh[:, ysl],
                                                ps_b[wv % 2][:, :], ALU.add))
                    last = None
                    for hh in range(8):
                        ha = wv * 8 + hh
                        cs = (wv % 8) * 512 + hh * 64
                        last = ve.scalar_tensor_tensor(
                            sb_yh[:, cs:cs + 64],
                            sb_hid[:, ha * 64:(ha + 1) * 64],
                            sb_dprc[:, ha:ha + 1],
                            sb_yh[:, cs:cs + 64], ALU.mult, ALU.add)
                    ev_y[wv] = X.inc(last, "ve")

            if wv % 8 == 7:
                hf = wv // 8
                with X.step() as blk:
                    @blk.vector
                    def _(ve, ci=ci, hf=hf):
                        X.wait(ve, h=X.c["h"], ve=X.c["ve"])
                        X.inc(ve.tensor_tensor(sb_yh[:, :], sb_yh[:, :],
                                               sb_sgh[:, :], ALU.mult), "ve")

                    for sg4 in range(4):
                        @blk.scalar
                        def _(se, ci=ci, sg4=sg4):
                            X.wait(se, ve=X.c["ve"])
                            X.edge(se, se.activation(
                                sb_t1[:, 0:1024],
                                sb_yh[:, sg4 * 1024:(sg4 + 1) * 1024],
                                AF.Square, accum_out=sb_qc[:, :]), "se")
                            se.activation(sb_sc[:, :], sb_qc[:, :], AF.Ln,
                                          scale=1.0 / 1024,
                                          bias=sb_eps[:, :])
                            X.inc(se.activation(sb_sc[:, :], sb_sc[:, :],
                                                AF.Exp, scale=-0.5), "se")

                        @blk.vector
                        def _(ve, ci=ci, sg4=sg4):
                            X.wait(ve, se=X.c["se"])
                            X.inc(ve.tensor_scalar(
                                sb_yh[:, sg4 * 1024:(sg4 + 1) * 1024],
                                sb_yh[:, sg4 * 1024:(sg4 + 1) * 1024],
                                sb_sc[:, :], None, ALU.mult), "ve")

                    @blk.gpsimd
                    def _(g, ci=ci, hf=hf, c0=c0):
                        X.wait(g, ve=X.c["ve"])
                        half_spill[hf] = X.dma(
                            g, sp_y[c0:c0 + CHK, hf * 4096:(hf + 1) * 4096],
                            sb_yh[:, :], "g")

                    if hf == 0:
                        @blk.sync
                        def _(sy, ci=ci, c0=c0):
                            X.wait(sy, ve=X.c["ve"])
                            X.dmat(sy, sb_sgh[:, :],
                                   sp_gate.ap()[4096:8192, c0:c0 + CHK])

        st_done = {}
        for wv in range(16):
            with X.step() as blk:
                @blk.vector
                def _(ve, ci=ci, wv=wv):
                    if wv >= 1:
                        X.wait(ve, pe=X.c["pe"])
                    par = wv % 2
                    last = None
                    for hh in range(8):
                        ha = wv * 8 + hh
                        last = ve.tensor_scalar(
                            sb_hd1[:, par * 512 + hh * 64:
                                   par * 512 + (hh + 1) * 64],
                            sb_hid[:, ha * 64:(ha + 1) * 64],
                            sb_dwT[:, ha:ha + 1], None, ALU.mult)
                    X.inc(last, "ve")

                @blk.tensor
                def _(te, ci=ci, wv=wv):
                    X.wait(te, ve=X.c["ve"])
                    par = wv % 2
                    last = None
                    for hh in range(8):
                        ha = wv * 8 + hh
                        last = te.matmul(
                            ps_b[4 + par][:, hh * 64:(hh + 1) * 64],
                            sb_Bn[:, (ha // 16) * 128:(ha // 16 + 1) * 128],
                            sb_hd1[:, par * 512 + hh * 64:
                                   par * 512 + (hh + 1) * 64],
                            start=(hh == 0), stop=True,
                            skip_group_check=True)
                    X.inc(last, "pe")

                @blk.vector
                def _(ve, ci=ci, wv=wv):
                    X.wait(ve, pe=X.c["pe"])
                    par = wv % 2
                    last = None
                    for hh in range(8):
                        ha = wv * 8 + hh
                        last = ve.scalar_tensor_tensor(
                            sb_Sbf[:, ha * 64:(ha + 1) * 64],
                            sb_Sbf[:, ha * 64:(ha + 1) * 64],
                            sb_EAr[:, ha:ha + 1],
                            ps_b[4 + par][:, hh * 64:(hh + 1) * 64],
                            ALU.mult, ALU.add)
                    st_done[wv] = X.inc(last, "ve")
        scan_done = st_done[15]

        if ci % OPB == OPB - 1:
            b0 = (ci - OPB + 1) * CHK
            bt0 = ci - OPB + 1
            with X.step() as blk:
                @blk.sync
                def _(sy, b0=b0):
                    X.wait(sy, pe=X.c["pe"], h=X.c["h"], g=X.c["g"])
                    X.dmat(sy, sb_mt.ap().rearrange(
                        "p (kt t) -> p kt t", kt=64),
                        sp_y.ap()[b0:b0 + OPB * CHK, :]
                        .rearrange("t (kt p) -> t kt p", p=128))
                m_yt = dict(X.c)

                @blk.vector
                def _(ve, b0=b0):
                    X.wait(ve, h=m_yt["h"])
                    last = None
                    for kt in range(64):
                        last = ve.tensor_scalar(
                            sb_mt[:, kt * OPB * CHK:(kt + 1) * OPB * CHK],
                            sb_mt[:, kt * OPB * CHK:(kt + 1) * OPB * CHK],
                            sb_gw[:, kt:kt + 1], None, ALU.mult)
                    X.inc(last, "ve")
            m_ytr = dict(X.c)

            wos_ld = {}
            mm4_done = {}
            ev_op = {}
            for dblk in range(8):
                with X.step() as blk:
                    for k4 in range(16):
                        @blk.gpsimd
                        def _(g, dblk=dblk, k4=k4):
                            pm = mm4_done.get((dblk, k4 - 2)) or \
                                mm4_done.get((dblk - 1, 14 + k4))
                            if pm:
                                X.wait(g, pe=pm)
                            else:
                                X.wait(g, ve=m_ytr["ve"])
                            wos_ld[(dblk, k4)] = X.dma(
                                g, sb_wos[k4 % 2].ap().rearrange(
                                    "p (k c) -> p k c", k=4),
                                sp_woT.ap()[k4 * 512:(k4 + 1) * 512,
                                            dblk * 512:(dblk + 1) * 512]
                                .rearrange("(k p) c -> p k c", p=128), "g")

                        @blk.tensor
                        def _(te, dblk=dblk, k4=k4):
                            if k4 == 0:
                                X.wait(te, ve=m_ytr["ve"])
                                if dblk >= 2:
                                    X.wait(te, ve=ev_op[dblk - 2])
                            X.wait(te, g=wos_ld[(dblk, k4)])
                            bank = (dblk % 2) * 4
                            last = None
                            for k in range(4):
                                kt = k4 * 4 + k
                                for tt in range(OPB):
                                    last = te.matmul(
                                        ps_b[bank + tt][:, :],
                                        sb_mt[:, kt * OPB * CHK + tt * 128:
                                              kt * OPB * CHK + (tt + 1) * 128],
                                        sb_wos[k4 % 2][:, k * 512:(k + 1) * 512],
                                        start=(kt == 0), stop=(kt == 63))
                            mm4_done[(dblk, k4)] = X.inc(last, "pe")
                    m_op = dict(X.c)

                    for tt in range(OPB):
                        @blk.gpsimd
                        def _(g, dblk=dblk, tt=tt, bt0=bt0):
                            pv = ev_op.get(dblk - 1)
                            if pv:
                                X.wait(g, ve=pv)
                            X.dma(g, sb_xsl[tt][:, :],
                                  xh[(bt0 + tt + 1) * 128:
                                     (bt0 + tt + 2) * 128,
                                     dblk * 512:(dblk + 1) * 512], "g")

                        @blk.scalar
                        def _(se, dblk=dblk, tt=tt, bt0=bt0):
                            X.wait(se, g=X.c["g"])
                            X.inc(se.activation(
                                sb_xsl[tt][:, :], sb_xsl[tt][:, :],
                                AF.Identity,
                                bias=sb_stats_nm[:, bt0 + tt + 1:bt0 + tt + 2],
                                scale=sb_stats_rp[:, bt0 + tt + 1:bt0 + tt + 2]),
                                "se")

                        @blk.vector
                        def _(ve, dblk=dblk, tt=tt):
                            bank = (dblk % 2) * 4
                            X.wait(ve, pe=m_op["pe"], se=X.c["se"])
                            if dblk >= 1:
                                X.wait(ve, h=X.c["h"])
                            X.inc(ve.tensor_tensor(
                                sb_x1s[tt][:, :], ps_b[bank + tt][:, :],
                                sb_xsl[tt][:, :], ALU.add), "ve")

                        @blk.sync
                        def _(sy, dblk=dblk, tt=tt, b0=b0):
                            X.wait(sy, ve=X.c["ve"])
                            X.dma(sy, sp_x1[b0 + tt * CHK:b0 + (tt + 1) * CHK,
                                            dblk * 512:(dblk + 1) * 512],
                                  sb_x1s[tt][:, :], "h")
                    ev_op[dblk] = X.c["ve"]
    m_b = dict(X.c)
    bb.close()
    ab.close()

    cc_ = X.st.enter_context(contextlib.ExitStack())
    sb_x1t = X.sb("c1_x1t", [128, D], F32, cc_)
    sb_x1n = X.sb("c1_x1n", [128, D], BF16, cc_)
    sb_sq2 = X.sb("c1_sq2", [128, D], F32, cc_)
    c2 = {k: X.sb("c1_" + k, [128, 1], F32, cc_)
          for k in ["sum", "sq", "m", "msq", "var", "r1", "t1", "nmb"]}
    for t in range(n_chunks):
        with X.step() as blk:
            @blk.gpsimd
            def _(g, t=t):
                X.wait(g, se=X.c["se"], h=m_b["h"] if t == 0 else X.c["h"])
                X.dma(g, sb_x1t[:, :], sp_x1[t * 128:(t + 1) * 128, :], "g")

            @blk.vector
            def _(ve, t=t):
                X.wait(ve, g=X.c["g"])
                X.inc(ve.tensor_reduce(c2["sum"][:, :], sb_x1t[:, :], AXX,
                                       ALU.add), "ve")

            @blk.scalar
            def _(se, t=t):
                X.wait(se, g=X.c["g"])
                X.edge(se, se.activation(sb_sq2[:, :], sb_x1t[:, :],
                                         AF.Square,
                                         accum_out=c2["sq"][:, :]), "se")
                X.wait(se, ve=X.c["ve"])
                se.activation(c2["m"][:, :], c2["sum"][:, :], AF.Identity,
                              scale=1.0 / D)
                se.activation(c2["msq"][:, :], c2["sq"][:, :], AF.Identity,
                              scale=1.0 / D)
                X.edge(se, se.activation(c2["msq"][:, :], c2["msq"][:, :],
                                         AF.Identity), "se")
                se.activation(c2["t1"][:, :], c2["m"][:, :], AF.Identity,
                              scale=c2["m"][:, :])
                X.edge(se, se.activation(c2["t1"][:, :], c2["t1"][:, :],
                                         AF.Identity), "se")
                se.activation(c2["var"][:, :], c2["t1"][:, :], AF.Identity,
                              scale=-1.0, bias=c2["msq"][:, :])
                se.activation(c2["t1"][:, :], c2["var"][:, :], AF.Ln,
                              bias=sb_eps[:, :])
                X.edge(se, se.activation(c2["r1"][:, :], c2["t1"][:, :],
                                         AF.Exp, scale=-0.5), "se")
                X.edge(se, se.activation(c2["nmb"][:, :], c2["m"][:, :],
                                         AF.Identity, scale=c2["r1"][:, :]),
                       "se")
                X.edge(se, se.activation(c2["nmb"][:, :], c2["nmb"][:, :],
                                         AF.Identity, scale=-1.0), "se")
                X.inc(se.activation(sb_x1n[:, :], sb_x1t[:, :], AF.Identity,
                                    bias=c2["nmb"][:, :],
                                    scale=c2["r1"][:, :]), "se")

            @blk.sync
            def _(sy, t=t):
                X.wait(sy, se=X.c["se"])
                X.dma(sy, sp_x1n[t * 128:(t + 1) * 128, :], sb_x1n[:, :], "h")
    m_c1 = dict(X.c)
    cc_.close()

    c3 = X.st.enter_context(contextlib.ExitStack())
    sb_xnT = X.sb("c3_xnT", [128, NKT * TOK], BF16, c3)
    sb_wfs = X.sb("c3_wfs", [128, NKT * 512], BF16, c3)
    sb_biasbf = X.sb("c3_biasbf", [1, D], BF16, c3)
    sb_xs2 = [X.sb(f"c3_xs{i}", [128, 512], F32, c3) for i in range(2)]
    sb_os2 = [X.sb(f"c3_os{i}", [128, 512], F32, c3) for i in range(2)]
    ps_c = [X.ps(f"c3_ps{i}", [128, 512], F32, c3) for i in range(2)]

    with X.step() as blk:
        @blk.gpsimd
        def _(g):
            X.dma(g, sb_biasbf[:, :], bias_f[:, :], "g")

        @blk.sync
        def _(sy):
            X.wait(sy, h=m_c1["h"])
            for q in range(4):
                X.dmat(sy, sb_xnT.ap()[:, q * 8 * TOK:(q + 1) * 8 * TOK]
                       .rearrange("p (kt t) -> p kt t", kt=8),
                       sp_x1n.ap()[0:TOK, q * 1024:(q + 1) * 1024]
                       .rearrange("t (kt p) -> t kt p", p=128))
    m_c2 = dict(X.c)

    wf_ld = {}
    ev_c = {}
    for dblk in range(8):
        with X.step() as blk:
            @blk.gpsimd
            def _(g, dblk=dblk):
                if dblk > 0:
                    X.wait(g, pe=X.c["pe"])
                for k4 in range(8):
                    wf_ld[(dblk, k4)] = X.dma(
                        g, sb_wfs.ap()[:, k4 * 2048:(k4 + 1) * 2048]
                        .rearrange("p (k c) -> p k c", k=4),
                        wfT.ap()[k4 * 512:(k4 + 1) * 512,
                                 dblk * 512:(dblk + 1) * 512]
                        .rearrange("(k p) c -> p k c", p=128), "g")

            for tt in range(n_chunks):
                @blk.gpsimd
                def _(g, dblk=dblk, tt=tt):
                    pv = ev_c.get((dblk, tt - 2)) or \
                        ev_c.get((dblk - 1, n_chunks - 2 + tt))
                    if pv:
                        X.wait(g, ve=pv)
                    X.dma(g, sb_xs2[tt % 2][:, :],
                          sp_x1[tt * 128:(tt + 1) * 128,
                                dblk * 512:(dblk + 1) * 512], "g")

                @blk.tensor
                def _(te, dblk=dblk, tt=tt):
                    if tt == 0:
                        X.wait(te, h=m_c2["h"], g=wf_ld[(dblk, 7)])
                    pv = ev_c.get((dblk, tt - 2)) or \
                        ev_c.get((dblk - 1, n_chunks - 2 + tt))
                    if pv:
                        X.wait(te, ve=pv)
                    for kt in range(NKT):
                        te.matmul(ps_c[tt % 2][:, :],
                                  sb_xnT[:, kt * TOK + tt * 128:
                                         kt * TOK + (tt + 1) * 128],
                                  sb_wfs[:, kt * 512:(kt + 1) * 512],
                                  start=(kt == 0), stop=False)
                    ins = te.matmul(ps_c[tt % 2][:, :], sb_ones_row[0:1, :],
                                    sb_biasbf[0:1, dblk * 512:(dblk + 1) * 512],
                                    start=False, stop=True)
                    X.inc(ins, "pe")

                @blk.vector
                def _(ve, dblk=dblk, tt=tt):
                    X.wait(ve, pe=X.c["pe"], g=X.c["g"])
                    if dblk + tt > 1:
                        X.wait(ve, h=X.c["h"])
                    ev_c[(dblk, tt)] = X.inc(ve.tensor_tensor(
                        sb_os2[tt % 2][:, :], ps_c[tt % 2][:, :],
                        sb_xs2[tt % 2][:, :], ALU.add), "ve")

                @blk.sync
                def _(sy, dblk=dblk, tt=tt):
                    X.wait(sy, ve=ev_c[(dblk, tt)])
                    X.dma(sy, out[tt * 128:(tt + 1) * 128,
                                  dblk * 512:(dblk + 1) * 512],
                          sb_os2[tt % 2][:, :], "h")
    if dbg:
        with X.step() as blk:
            @blk.gpsimd
            def _(g):
                X.wait(g, h=X.c["h"], ve=X.c["ve"], se=X.c["se"],
                       pe=X.c["pe"])
                for name, t in [("sp_dbg9", sp_dbg9), ("sp_dbgA", sp_dbgA),
                                ("sp_dbgB", sp_dbgB), ("sp_dbgC", sp_dbgC),
                                ("sp_dbgD", sp_dbgD),
                                ("sp_dbg3", sp_dbg3), ("sp_dbg4", sp_dbg4),
                                ("sp_dbg5", sp_dbg5), ("sp_dbg6", sp_dbg6),
                                ("sp_dbg7", sp_dbg7), ("sp_dbg8", sp_dbg8),
                                ("sp_xn", sp_xn), ("sp_hbc", sp_hbc),
                                ("sp_gate", sp_gate), ("sp_acs", sp_acs),
                                ("sp_y", sp_y), ("sp_x1", sp_x1),
                                ("sp_x1n", sp_x1n), ("rel_out", rel_out)]:
                    tp = nc.dram_tensor("tap_" + name, list(t.shape),
                                        t.dtype if hasattr(t, "dtype") else F32,
                                        kind="ExternalOutput")
                    X.dma(g, tp.ap(), t.ap(), "g")
                g.wait_ge(X.sem["g"], X.c["g"])
    with X.step() as blk:
        @blk.sync
        def _(sy):
            X.wait(sy, h=X.c["h"])
    c3.close()
    return nc


def _relay_mats(q):
    Gm = np.zeros((4, 4), np.float32)
    Mm = np.zeros(4, np.float32)
    for r in range(4):
        if r < q:
            Mm[r] = 1.0
            for m in range(r + 1, q):
                Gm[r, m] = 1.0
    relG = np.broadcast_to(Gm.reshape(1, 16), (128, 16)).copy()
    relM = np.broadcast_to(Mm.reshape(1, 4), (128, 4)).copy()
    return np.ascontiguousarray(relG), np.ascontiguousarray(relM)


_NC_CACHE = {}


def make_in_maps(n_chunks, L0_attn_x, L0_attn_norm_w, L0_attn_in_proj_w,
                 L0_attn_conv_w, L0_attn_conv_b, L0_attn_dt_bias,
                 L0_attn_A_log, L0_attn_Dp, L0_attn_gnorm_w,
                 L0_attn_out_proj_w, L0_ffn_weight, L0_ffn_bias):
    TOK = CHK * n_chunks
    x = np.asarray(L0_attn_x, np.float32)
    Bq, S, _ = x.shape
    assert S == 4 * TOK and Bq == 2

    wiT_p = np.ascontiguousarray(
        np.concatenate([
            np.asarray(L0_attn_in_proj_w[DIN + CONV:, :], np.float32).T,
            np.asarray(L0_attn_in_proj_w[DIN:DIN + CONV, :], np.float32).T,
            np.asarray(L0_attn_in_proj_w[:DIN, :], np.float32).T,
        ], axis=1))
    woT = np.ascontiguousarray(np.asarray(L0_attn_out_proj_w, np.float32).T)
    wfT = np.ascontiguousarray(np.asarray(L0_ffn_weight, np.float16).T)
    bias_f = np.ascontiguousarray(np.asarray(L0_ffn_bias, np.float16)
                                  .reshape(1, D))
    convw = np.ascontiguousarray(
        np.asarray(L0_attn_conv_w, np.float32).reshape(NB_HBC, 128, KC)
        .transpose(1, 0, 2).reshape(128, NB_HBC * KC))
    convb = np.ascontiguousarray(
        np.asarray(L0_attn_conv_b, np.float32).reshape(NB_HBC, 128).T)
    dtb = np.asarray(L0_attn_dt_bias, np.float32).reshape(128, 1)
    alog = np.asarray(L0_attn_A_log, np.float32).reshape(128, 1)
    dpv = np.asarray(L0_attn_Dp, np.float32).reshape(128, 1)
    nwv = np.ascontiguousarray(
        np.asarray(L0_attn_norm_w, np.float32).reshape(NKT, 128).T)
    gwv = np.ascontiguousarray(
        np.asarray(L0_attn_gnorm_w, np.float32).reshape(DIN // 128, 128).T)

    in_maps = []
    for cid in range(8):
        b, q = cid // 4, cid % 4
        t0 = q * TOK
        xh = np.zeros(((n_chunks + 1) * 128, D), np.float32)
        if t0 >= 3:
            xh[125:128] = x[b, t0 - 3:t0]
        xh[128:128 + TOK] = x[b, t0:t0 + TOK]
        relG, relM = _relay_mats(q)
        in_maps.append({
            "xh": np.ascontiguousarray(xh), "wiT": wiT_p, "woT": woT,
            "wfT": wfT, "bias_f": bias_f, "convw": convw, "convb": convb,
            "dtb": dtb, "alog": alog, "dpv": dpv, "nwv": nwv, "gwv": gwv,
            "relG": relG, "relM": relM,
        })
    return in_maps


def kernel(**inputs):
    n_chunks = 16
    TOK = CHK * n_chunks
    S = 4 * TOK
    in_maps = make_in_maps(n_chunks, **inputs)
    if n_chunks not in _NC_CACHE:
        _NC_CACHE[n_chunks] = build_program(n_chunks)
    nc = _NC_CACHE[n_chunks]
    res = run_bass_kernel_spmd(nc, in_maps, core_ids=list(range(8)))
    outp = np.empty((2, S, D), np.float32)
    for cid in range(8):
        b, q = cid // 4, cid % 4
        outp[b, q * TOK:(q + 1) * TOK] = res.results[cid]["out"]
    return outp


if __name__ == "__main__":
    build_program(2)
    print("build ok")
